# version 04
# baseline (speedup 1.0000x reference)
import numpy as np
from contextlib import ExitStack

import concourse.bass as bass
import concourse.mybir as mybir
from concourse.bass_utils import run_bass_kernel_spmd

F32 = mybir.dt.float32
BF16 = mybir.dt.bfloat16
AF = mybir.ActivationFunctionType
ALU = mybir.AluOpType
AX = mybir.AxisListType

D = 1024
HEADS = 16
DH = 64
WINDOW = 256
INNER = 2730
QK_SCALE = 8.0
NEG = -30000.0
LN_EPS = 1e-5

T_OWN = 4096
N_OWN = T_OWN // 128
N_HALO = 4



class Sem:
    __slots__ = ("h", "n")

    def __init__(self, handle):
        self.h = handle
        self.n = 0


class Prog:
    ENGINES = ("sync", "scalar", "vector", "tensor", "gpsimd")

    def __init__(self, stack):
        self.nc = bass.Bass("TRN2", target_bir_lowering=False, debug=False)
        self.stack = stack
        self.streams = {e: [] for e in self.ENGINES}
        self._id = 0

    def sem(self, name=None):
        self._id += 1
        h = self.stack.enter_context(self.nc.semaphore(name or f"s{self._id}"))
        return Sem(h)

    def sbuf(self, name, shape, dtype):
        return self.stack.enter_context(self.nc.sbuf_tensor(name, list(shape), dtype))

    def psum(self, name, shape, dtype):
        return self.stack.enter_context(self.nc.psum_tensor(name, list(shape), dtype))

    def wait(self, engine, sem, val):
        if val > 0:
            self.streams[engine].append(lambda e, s=sem.h, v=val: e.wait_ge(s, v))

    def op(self, engine, fn, sem=None, delta=1):
        if sem is not None:
            sem.n += delta

        def run(e, fn=fn, sem=sem, delta=delta):
            ins = fn(e)
            if sem is not None:
                ins.then_inc(sem.h, delta)
        self.streams[engine].append(run)

    def dma(self, engine, out, in_, sem=None):
        self.op(engine, lambda e, o=out, i=in_: e.dma_start(out=o, in_=i),
                sem=sem, delta=16)

    def build(self):
        block = self.stack.enter_context(self.nc.Block())
        for name in self.ENGINES:
            stream = self.streams[name]
            if not stream:
                continue

            def body(eng, stream=stream):
                for f in stream:
                    f(eng)
            getattr(block, name)(body)
        return self.nc


def bcast_load(p, dst_sb, src_row, sem, rows=128):
    n = dst_sb.tensor.shape[1]
    p.dma("sync", dst_sb[:], src_row.broadcast_to([rows, n]), sem=sem)


class WLoader:

    def __init__(self, p, name):
        self.p = p
        self.stage = [p.sbuf(f"{name}_stg{i}", [128, 512], F32) for i in range(2)]
        self.s_ld = [p.sem(), p.sem()]
        self.s_cast = Sem(None)
        self.s_cast = p.sem()
        self.i = 0

    def load(self, dst_ap, src_ap, rows, cols):
        p, i = self.p, self.i
        buf = i % 2
        st = self.stage[buf]
        if i >= 2:
            p.wait("sync", self.s_cast, i - 1)
        p.dma("sync", st[0:rows, 0:cols], src_ap, sem=self.s_ld[buf])
        p.wait("scalar", self.s_ld[buf], 16 * (i // 2 + 1))
        p.op("scalar", lambda e, d=dst_ap, s=st, r=rows, c=cols:
             e.activation(out=d, in_=s[0:r, 0:c], func=AF.Copy), sem=self.s_cast)
        self.i += 1

    def load_matrix(self, dst_chunk_fn, src, nrows, ncols, colstep=512):
        kchunks = (nrows + 127) // 128
        for kk in range(kchunks):
            r = min(128, nrows - kk * 128)
            for c0 in range(0, ncols, colstep):
                c = min(colstep, ncols - c0)
                self.load(dst_chunk_fn(kk)[0:r, c0:c0 + c],
                          src[kk * 128:kk * 128 + r, c0:c0 + c], r, c)


def emit_ln_stats(p, sz_ap, stat, sv, sa, scratch, eps):
    p.wait("vector", sv, sv.n)
    p.wait("vector", sa, sa.n)
    p.op("vector", lambda e: e.reduce_sum(out=stat[:, 0:1], in_=sz_ap, axis=AX.X), sem=sv)
    p.wait("scalar", sv, sv.n - 1)
    p.wait("scalar", sa, sa.n)
    p.op("scalar", lambda e: e.activation(out=scratch, in_=sz_ap, func=AF.Square,
                                          accum_out=stat[:, 1:2]), sem=sa)
    p.wait("vector", sa, sa.n)
    p.wait("vector", sv, sv.n)
    p.op("vector", lambda e: e.tensor_scalar(out=stat[:, 2:3], in0=stat[:, 0:1],
                                             scalar1=1.0 / D, scalar2=None,
                                             op0=ALU.mult), sem=sv)
    p.op("vector", lambda e: e.tensor_scalar(out=stat[:, 3:4], in0=stat[:, 1:2],
                                             scalar1=1.0 / D, scalar2=None,
                                             op0=ALU.mult), sem=sv)
    p.wait("vector", sv, sv.n)
    p.op("vector", lambda e: e.tensor_tensor(out=stat[:, 5:6], in0=stat[:, 2:3],
                                             in1=stat[:, 2:3], op=ALU.mult), sem=sv)
    p.wait("vector", sv, sv.n)
    p.op("vector", lambda e: e.tensor_tensor(out=stat[:, 3:4], in0=stat[:, 3:4],
                                             in1=stat[:, 5:6], op=ALU.subtract), sem=sv)
    p.wait("vector", sv, sv.n)
    p.op("vector", lambda e, b=eps: e.tensor_scalar(out=stat[:, 3:4], in0=stat[:, 3:4],
                                                    scalar1=b, scalar2=None,
                                                    op0=ALU.add), sem=sv)
    p.wait("scalar", sv, sv.n)
    p.wait("scalar", sa, sa.n)
    p.op("scalar", lambda e: e.activation(out=stat[:, 6:7], in_=stat[:, 3:4],
                                          func=AF.Ln), sem=sa)
    p.wait("scalar", sa, sa.n)
    p.op("scalar", lambda e: e.activation(out=stat[:, 4:5], in_=stat[:, 6:7],
                                          func=AF.Exp, scale=-0.5), sem=sa)



def build_ffnsub(final: bool, n_own=N_OWN):
    stack = ExitStack()
    with stack:
        p = Prog(stack)
        nc = p.nc
        T = n_own * 128
        xmid = nc.dram_tensor("xmid", [T + 1, D], F32, kind="ExternalInput").ap()
        w1 = nc.dram_tensor("w1", [D, 2 * INNER], F32, kind="ExternalInput").ap()
        w2 = nc.dram_tensor("w2", [INNER, D], F32, kind="ExternalInput").ap()
        lng = nc.dram_tensor("lng", [1, D], F32, kind="ExternalInput").ap()
        lnb = nc.dram_tensor("lnb", [1, D], F32, kind="ExternalInput").ap()
        b1d = nc.dram_tensor("b1", [1, 2 * INNER], F32, kind="ExternalInput").ap()
        b2d = nc.dram_tensor("b2", [1, D], F32, kind="ExternalInput").ap()
        identd = nc.dram_tensor("ident", [128, 128], BF16, kind="ExternalInput").ap()
        if final:
            x1d = nc.dram_tensor("x1", [T, D], F32, kind="ExternalInput").ap()
        zout = nc.dram_tensor("zout", [T, D], F32, kind="ExternalOutput").ap()

        w1_sb = p.sbuf("w1_sb", [128, 8 * 2 * INNER], BF16)
        w2_sb = p.sbuf("w2_sb", [128, 22 * D], BF16)
        id_sb = p.sbuf("id_sb", [128, 128], BF16)
        g_b = p.sbuf("g_b", [128, D], BF16)
        b_b = p.sbuf("b_b", [128, D], BF16)
        b1_b = p.sbuf("b1_b", [128, 2 * INNER], BF16)
        b2_b = p.sbuf("b2_b", [128, D], BF16)
        sz = [p.sbuf(f"sz{i}", [128, D], F32) for i in range(2)]
        xal = [p.sbuf(f"xal{i}", [128, D], F32) for i in range(2)]
        x1al = p.sbuf("x1al", [128, D], F32) if final else None
        t1 = p.sbuf("t1", [128, D], F32)
        xn_bf = p.sbuf("xn_bf", [128, D], BF16)
        xnT = p.sbuf("xnT", [128, D], BF16)
        stat = p.sbuf("stat", [128, 8], F32)
        stat2 = p.sbuf("stat2", [128, 8], F32) if final else None
        gtmp = p.sbuf("gtmp", [128, 512], F32)
        atmp = p.sbuf("atmp", [128, 512], F32)
        gel = p.sbuf("gel", [128, 512], BF16)
        h_sb = p.sbuf("h_sb", [128, INNER], BF16)
        hT = p.sbuf("hT", [128, 22 * 128], BF16)
        yout = [p.sbuf(f"yout{i}", [128, D], F32) for i in range(2)]

        tp_ps = [p.psum(f"tp{i}", [128, 1024], BF16) for i in range(2)]
        ag_ps = [p.psum(f"ag{i}", [128, 512], F32) for i in range(4)]
        y2_ps = p.psum("y2", [128, 1024], F32)

        s_cload = p.sem()
        s_xld = [p.sem(), p.sem()]
        sv = p.sem()
        sa = p.sem()
        s_xn = p.sem()
        s_xnT = p.sem()
        s_xnTe = p.sem()
        s_w1mm = p.sem()
        s_gate = p.sem()
        s_hT = p.sem()
        s_hTe = p.sem()
        s_w2mm = p.sem()
        s_y = p.sem()
        s_st = p.sem()

        wl = WLoader(p, "w")

        p.dma("sync", id_sb[:], identd, sem=s_cload)
        n_cload = 1 * 16
        for c0 in range(0, 2 * INNER, 512):
            c = min(512, 2 * INNER - c0)
            wl.load(b1_b[:, c0:c0 + c], b1d[0:1, c0:c0 + c].broadcast_to([128, c]), 128, c)
        for dst, srcd in ((g_b, lng), (b_b, lnb), (b2_b, b2d)):
            for c0 in range(0, D, 512):
                wl.load(dst[:, c0:c0 + 512], srcd[0:1, c0:c0 + 512].broadcast_to([128, 512]),
                        128, 512)
        wl.load_matrix(lambda kk: w1_sb[:, kk * 2 * INNER:(kk + 1) * 2 * INNER],
                       w1, D, 2 * INNER)
        wl.load_matrix(lambda kk: w2_sb[:, kk * D:(kk + 1) * D], w2, INNER, D)
        n_wcast = wl.s_cast.n

        col_off = [0, 512, 1024, 1536, 2048, 2560]
        col_sz = [512, 512, 512, 512, 512, INNER - 2560]
        w2_rows = [min(128, INNER - kk * 128) for kk in range(22)]
        ny = 3 if final else 2

        for j in range(n_own):
            jb = j % 2
            if j >= 2:
                p.wait("sync", s_xn, j - 1)
                p.wait("sync", s_y, ny * (j - 1))
            p.dma("sync", xal[jb][:], xmid[1 + 128 * j:1 + 128 * (j + 1), :], sem=s_xld[jb])
            p.dma("sync", sz[jb][:, 512:D], xmid[128 * j:128 * (j + 1), 512:D], sem=s_xld[jb])
            p.dma("sync", sz[jb][:, 0:512], xmid[1 + 128 * j:1 + 128 * (j + 1), 0:512],
                  sem=s_xld[jb])
            if final:
                p.wait("sync", s_y, ny * j)
                p.dma("sync", x1al[:], x1d[128 * j:128 * (j + 1), :], sem=s_xld[jb])
            nld = (4 if final else 3) * 16 * (j // 2 + 1)

            p.wait("vector", s_xld[jb], nld)
            p.wait("scalar", s_xld[jb], nld)
            p.wait("scalar", s_y, ny * j)
            if final:
                p.wait("scalar", s_st, 16 * j)
                p.wait("vector", s_st, 16 * j)
            emit_ln_stats(p, sz[jb][:], stat, sv, sa, t1[:], LN_EPS)
            p.wait("vector", sa, sa.n)
            p.op("vector", lambda e, b=jb: e.tensor_scalar(
                out=t1[:], in0=sz[b][:], scalar1=stat[:, 2:3], scalar2=stat[:, 4:5],
                op0=ALU.subtract, op1=ALU.mult), sem=sv)
            p.wait("vector", sv, sv.n)
            p.op("vector", lambda e: e.tensor_tensor(out=t1[:], in0=t1[:], in1=g_b[:],
                                                     op=ALU.mult), sem=sv)
            p.wait("vector", sv, sv.n)
            p.wait("vector", s_xnT, j)
            p.op("vector", lambda e: e.tensor_tensor(out=xn_bf[:], in0=t1[:], in1=b_b[:],
                                                     op=ALU.add), sem=s_xn)

            p.wait("tensor", s_xn, j + 1)
            if j == 0:
                p.wait("tensor", s_cload, n_cload)
            p.wait("tensor", s_hTe, s_hTe.n)
            for kk in range(8):
                p.op("tensor", lambda e, k=kk: e.transpose(
                    tp_ps[0][:, k * 128:(k + 1) * 128], xn_bf[:, k * 128:(k + 1) * 128],
                    id_sb[:]), sem=(s_xnT if kk == 7 else None))
            p.wait("scalar", s_xnT, s_xnT.n)
            p.wait("scalar", s_w1mm, 6 * j)
            p.op("scalar", lambda e: e.activation(out=xnT[:], in_=tp_ps[0][:],
                                                  func=AF.Copy), sem=s_xnTe)

            if j == 0:
                p.wait("tensor", wl.s_cast, n_wcast)
            p.wait("tensor", s_xnTe, s_xnTe.n)
            for c in range(6):
                g_idx = j * 6 + c
                pa = ag_ps[2 * (c % 2)]
                pg = ag_ps[2 * (c % 2) + 1]
                if g_idx >= 2:
                    p.wait("tensor", s_gate, g_idx - 1)
                cw = col_sz[c]
                for kk in range(8):
                    p.op("tensor", lambda e, k=kk, pa=pa, c0=col_off[c], cw=cw: e.matmul(
                        pa[:, 0:cw], xnT[:, k * 128:(k + 1) * 128],
                        w1_sb[:, k * 2 * INNER + c0:k * 2 * INNER + c0 + cw],
                        start=(k == 0), stop=(k == 7)), sem=None)
                for kk in range(8):
                    p.op("tensor", lambda e, k=kk, pg=pg, c0=INNER + col_off[c], cw=cw:
                         e.matmul(pg[:, 0:cw], xnT[:, k * 128:(k + 1) * 128],
                                  w1_sb[:, k * 2 * INNER + c0:k * 2 * INNER + c0 + cw],
                                  start=(k == 0), stop=(k == 7)),
                         sem=(s_w1mm if kk == 7 else None))
                p.wait("vector", s_w1mm, s_w1mm.n)
                if c == 0:
                    p.wait("vector", s_hT, 3 * j)
                p.wait("vector", sa, sa.n)
                p.op("vector", lambda e, pg=pg, c0=INNER + col_off[c], cw=cw: e.tensor_tensor(
                    out=gtmp[:, 0:cw], in0=pg[:, 0:cw], in1=b1_b[:, c0:c0 + cw],
                    op=ALU.add), sem=sv)
                p.wait("scalar", sv, sv.n)
                p.wait("scalar", s_gate, s_gate.n)
                p.op("scalar", lambda e, cw=cw: e.activation(
                    out=gel[:, 0:cw], in_=gtmp[:, 0:cw], func=AF.Gelu), sem=sa)
                p.wait("vector", s_gate, s_gate.n)
                p.op("vector", lambda e, pa=pa, c0=col_off[c], cw=cw: e.tensor_tensor(
                    out=atmp[:, 0:cw], in0=pa[:, 0:cw], in1=b1_b[:, c0:c0 + cw],
                    op=ALU.add), sem=sv)
                p.wait("vector", sa, sa.n)
                p.wait("vector", sv, sv.n)
                p.op("vector", lambda e, c0=col_off[c], cw=cw: e.tensor_tensor(
                    out=h_sb[:, c0:c0 + cw], in0=atmp[:, 0:cw], in1=gel[:, 0:cw],
                    op=ALU.mult), sem=s_gate)

            p.wait("tensor", s_gate, 6 * (j + 1))
            groups = [(0, 8, 0), (8, 16, 1), (16, 22, 0)]
            p.wait("tensor", s_xnTe, s_xnTe.n)
            first_evac_done = False
            for (a0, a1, pi) in groups:
                if pi == 0 and a0 == 16:
                    p.wait("tensor", s_hTe, s_hTe.n - 1)
                for kk in range(a0, a1):
                    r = 128 if kk < 21 else INNER - 21 * 128
                    p.op("tensor", lambda e, k=kk, pi=pi, a0=a0, r=r: e.transpose(
                        tp_ps[pi][0:r, (k - a0) * 128:(k - a0 + 1) * 128],
                        h_sb[:, k * 128:k * 128 + r], id_sb[:]),
                        sem=(s_hT if kk == a1 - 1 else None))
                p.wait("scalar", s_hT, s_hT.n)
                if not first_evac_done:
                    p.wait("scalar", s_w2mm, 2 * j)
                    first_evac_done = True
                p.op("scalar", lambda e, a0=a0, a1=a1, pi=pi: e.activation(
                    out=hT[:, a0 * 128:a1 * 128], in_=tp_ps[pi][:, 0:(a1 - a0) * 128],
                    func=AF.Copy), sem=s_hTe)

            p.wait("tensor", s_hTe, s_hTe.n)
            p.wait("tensor", s_y, ny * j)
            for c in range(2):
                for kk in range(22):
                    r = w2_rows[kk]
                    p.op("tensor", lambda e, k=kk, c=c, r=r: e.matmul(
                        y2_ps[:, c * 512:(c + 1) * 512], hT[0:r, k * 128:(k + 1) * 128],
                        w2_sb[0:r, k * D + c * 512:k * D + (c + 1) * 512],
                        start=(k == 0), stop=(k == 21)),
                        sem=(s_w2mm if kk == 21 else None))

            p.wait("vector", s_w2mm, s_w2mm.n)
            p.wait("vector", s_st, 16 * max(0, j - 1))
            dst = yout[jb]
            p.wait("vector", s_xn, s_xn.n)
            p.op("vector", lambda e, b=jb: e.tensor_tensor(out=t1[:], in0=xal[b][:],
                                                           in1=b2_b[:], op=ALU.add), sem=sv)
            p.wait("vector", sv, sv.n)
            for c in range(2):
                p.op("vector", lambda e, c=c, d=dst: e.tensor_tensor(
                    out=d[:, c * 512:(c + 1) * 512], in0=y2_ps[:, c * 512:(c + 1) * 512],
                    in1=t1[:, c * 512:(c + 1) * 512], op=ALU.add), sem=s_y)

            if final:
                p.wait("vector", s_y, s_y.n)
                p.wait("scalar", s_y, s_y.n)
                p.wait("scalar", s_st, 16 * j)
                p.wait("vector", s_st, 16 * j)
                emit_ln_stats(p, yout[jb][:], stat2, sv, sa, t1[:], LN_EPS)
                p.wait("vector", sa, sa.n)
                p.op("vector", lambda e, b=jb: e.tensor_scalar(
                    out=t1[:], in0=yout[b][:], scalar1=stat2[:, 2:3], scalar2=stat2[:, 4:5],
                    op0=ALU.subtract, op1=ALU.mult), sem=sv)
                p.wait("vector", sv, sv.n)
                p.op("vector", lambda e: e.tensor_tensor(
                    out=t1[:], in0=t1[:], in1=x1al[:], op=ALU.add), sem=s_y)

            p.wait("gpsimd", s_y, s_y.n)
            src_tile = (t1 if final else yout[jb])
            p.dma("gpsimd", zout[128 * j:128 * (j + 1), :], src_tile[:], sem=s_st)

        p.wait("sync", s_st, s_st.n)
        nc_built = p.build()
    return nc_built



def build_attnsub(n_own=N_OWN, n_halo=N_HALO):
    NT = n_halo + n_own
    stack = ExitStack()
    with stack:
        p = Prog(stack)
        nc = p.nc
        z_ext = nc.dram_tensor("z_ext", [NT * 128 + 1, D], F32, kind="ExternalInput").ap()
        wqkv = nc.dram_tensor("wqkv", [D, 3 * D], F32, kind="ExternalInput").ap()
        wo = nc.dram_tensor("wo", [D, D], F32, kind="ExternalInput").ap()
        qsd = nc.dram_tensor("qs", [1, DH], F32, kind="ExternalInput").ap()
        ksd = nc.dram_tensor("ks", [1, DH], F32, kind="ExternalInput").ap()
        tabs = {nm: nc.dram_tensor(nm, [NT * 128, DH], F32, kind="ExternalInput").ap()
                for nm in ("cs", "ss", "ci", "si")}
        masksd = nc.dram_tensor("masks", [6 * 128, 512], F32, kind="ExternalInput").ap()
        identd = nc.dram_tensor("ident", [128, 128], BF16, kind="ExternalInput").ap()
        xmidd = nc.dram_tensor("xmid", [n_own * 128, D], F32, kind="ExternalOutput").ap()

        wqkv_sb = p.sbuf("wqkv_sb", [128, 8 * 3 * D], BF16)
        wo_sb = p.sbuf("wo_sb", [128, 8 * D], BF16)
        id_sb = p.sbuf("id_sb", [128, 128], BF16)
        qs_b = p.sbuf("qs_b", [128, DH], F32)
        ks_b = p.sbuf("ks_b", [128, DH], F32)
        qsr_b = p.sbuf("qsr_b", [128, DH], F32)
        ksr_b = p.sbuf("ksr_b", [128, DH], F32)
        tab_sb = {nm: p.sbuf(f"tab_{nm}", [128, NT * DH], BF16) for nm in
                  ("cs", "ss", "ci", "si")}
        mask_sb = p.sbuf("mask_sb", [128, 6 * 512], F32)
        zal = [p.sbuf(f"zal{i}", [128, D], F32) for i in range(2)]
        zsh = [p.sbuf(f"zsh{i}", [128, 512], F32) for i in range(2)]
        zal_bf = p.sbuf("zal_bf", [128, D], BF16)
        zsh_bf = p.sbuf("zsh_bf", [128, 512], BF16)
        zT = p.sbuf("zT", [128, D], BF16)
        q_f = p.sbuf("q_f", [128, D], F32)
        k_f = p.sbuf("k_f", [128, D], F32)
        qsq = p.sbuf("qsq", [128, D], F32)
        ssq = p.sbuf("ssq", [128, 2 * HEADS], F32)
        q1 = p.sbuf("q1", [128, D], BF16)
        k1 = p.sbuf("k1", [128, D], BF16)
        rot = p.sbuf("rot", [128, D], BF16)
        tq1 = p.sbuf("tq1", [128, D], BF16)
        qf_bf = p.sbuf("qf_bf", [128, D], BF16)
        kf_bf = p.sbuf("kf_bf", [128, D], BF16)
        qT = p.sbuf("qT", [128, D], BF16)
        kcache = p.sbuf("kcache", [128, 8 * 512], BF16)
        vcache = p.sbuf("vcache", [128, 4 * D], BF16)
        P_sb = [p.sbuf(f"P{i}", [128, 512], BF16) for i in range(2)]
        ptT = [p.sbuf(f"ptT{i}", [128, 512], BF16) for i in range(2)]
        nmax = [p.sbuf(f"nmax{i}", [128, 1], F32) for i in range(2)]
        sume = [p.sbuf(f"sume{i}", [128, 1], F32) for i in range(2)]
        rs = [p.sbuf(f"rs{i}", [128, 1], F32) for i in range(2)]
        ao_sb = p.sbuf("ao_sb", [128, D], BF16)
        aoT = p.sbuf("aoT", [128, D], BF16)
        xmid_sb = [p.sbuf(f"xm{i}", [128, D], F32) for i in range(2)]

        qkv_ps = [p.psum(f"qkv{i}", [128, 512], F32) for i in range(2)]
        tp_ps = p.psum("tp", [128, 1024], BF16)
        sim_ps = [p.psum(f"sim{i}", [128, 512], F32) for i in range(2)]
        pt_ps = p.psum("pt", [128, 1024], BF16)
        pv_ps = [p.psum(f"pv{i}", [128, 512], F32) for i in range(2)]

        s_cload = p.sem()
        s_tfold = p.sem()
        s_zld = [p.sem(), p.sem()]
        sv = p.sem()
        sa = p.sem()
        s_zcast = p.sem()
        s_zT = p.sem()
        s_zTe = p.sem()
        s_qkv = p.sem()
        s_qkev = p.sem()
        s_vev = p.sem()
        s_qproc = p.sem()
        s_qT = p.sem()
        s_qTe = p.sem()
        s_kT = p.sem()
        s_kTe = p.sem()
        s_sim = p.sem()
        s_mask = p.sem()
        s_max = p.sem()
        s_exp = p.sem()
        s_pt = p.sem()
        s_pte = p.sem()
        s_rcp = p.sem()
        s_pv = p.sem()
        s_ao = p.sem()
        s_aoT = p.sem()
        s_aoTe = p.sem()
        s_womm = p.sem()
        s_xm = p.sem()
        s_st = [p.sem(), p.sem()]

        wl = WLoader(p, "w")

        p.dma("sync", id_sb[:], identd, sem=s_cload)
        p.dma("sync", qs_b[:], qsd.broadcast_to([128, DH]), sem=s_cload)
        p.dma("sync", ks_b[:], ksd.broadcast_to([128, DH]), sem=s_cload)
        p.dma("sync", mask_sb[:].rearrange("p (m c) -> p m c", c=512),
              masksd.rearrange("(m p) c -> p m c", p=128), sem=s_cload)
        n_cload = 4 * 16
        tab_f32 = p.sbuf("tab_f32", [128, NT * DH], F32)
        s_tld = [p.sem(), p.sem()]
        for i, nm in enumerate(("cs", "ss", "ci", "si")):
            p.wait("sync", s_tfold, i)
            p.dma("sync", tab_f32[:].rearrange("p (t d) -> p t d", d=DH),
                  tabs[nm].rearrange("(t p) d -> p t d", p=128), sem=s_tld[i % 2])
            if i == 0:
                p.wait("scalar", s_cload, n_cload)
                for (dst, srcb) in ((qsr_b, qs_b), (ksr_b, ks_b)):
                    p.op("scalar", lambda e, d=dst, s=srcb: e.activation(
                        out=d[:, 0:32], in_=s[:, 32:64], func=AF.Copy), sem=sa)
                    p.op("scalar", lambda e, d=dst, s=srcb: e.activation(
                        out=d[:, 32:64], in_=s[:, 0:32], func=AF.Copy), sem=sa)
            sc = (qs_b, qsr_b, ks_b, ksr_b)[i]
            p.wait("vector", s_tld[i % 2], 16 * (i // 2 + 1))
            p.wait("vector", sa, 4)
            p.wait("vector", s_cload, n_cload)
            p.op("vector", lambda e, nm=nm, sc=sc: e.tensor_tensor(
                out=tab_sb[nm][:].rearrange("p (t d) -> p t d", d=DH),
                in0=tab_f32[:].rearrange("p (t d) -> p t d", d=DH),
                in1=sc.rearrange("p (o d) -> p o d", o=1).broadcast_to([128, NT, DH]),
                op=ALU.mult), sem=s_tfold)

        def wq_chunk(kk):
            return wqkv_sb[:, kk * 3 * D:(kk + 1) * 3 * D]
        wl.load_matrix(wq_chunk, wqkv, D, 3 * D)
        wl.load_matrix(lambda kk: wo_sb[:, kk * D:(kk + 1) * D], wo, D, D)
        n_wcast = wl.s_cast.n

        n_att = 0
        for j in range(NT):
            jb = j % 2
            attn = (j >= n_halo)
            slot = j % 4
            if j >= 2:
                p.wait("sync", s_zT, j - 1)
                if attn:
                    p.wait("sync", s_xm, 2 * max(0, n_att - 1))
            p.dma("sync", zal[jb][:], z_ext[1 + 128 * j:1 + 128 * (j + 1), :], sem=s_zld[jb])
            p.dma("sync", zsh[jb][:], z_ext[128 * j:128 * (j + 1), 512:D], sem=s_zld[jb])
            nld = 32 * (j // 2 + 1)

            p.wait("scalar", s_zld[jb], nld)
            p.wait("scalar", s_zT, j)
            p.op("scalar", lambda e, b=jb: e.activation(out=zal_bf[:], in_=zal[b][:],
                                                        func=AF.Copy), sem=s_zcast)
            p.op("scalar", lambda e, b=jb: e.activation(out=zsh_bf[:], in_=zsh[b][:],
                                                        func=AF.Copy), sem=s_zcast)
            p.wait("tensor", s_zcast, s_zcast.n)
            if j == 0:
                p.wait("tensor", s_cload, n_cload)
                p.wait("tensor", wl.s_cast, n_wcast)
            p.wait("tensor", s_zTe, s_zTe.n)
            for kk in range(8):
                src_ap = (zal_bf[:, kk * 128:(kk + 1) * 128] if kk < 4
                          else zsh_bf[:, (kk - 4) * 128:(kk - 3) * 128])
                p.op("tensor", lambda e, s=src_ap, k=kk: e.transpose(
                    tp_ps[:, k * 128:(k + 1) * 128], s, id_sb[:]),
                    sem=(s_zT if kk == 7 else None))
            p.wait("scalar", s_zT, s_zT.n)
            p.wait("scalar", s_qkv, s_qkv.n)
            p.op("scalar", lambda e: e.activation(out=zT[:], in_=tp_ps[:],
                                                  func=AF.Copy), sem=s_zTe)

            cols = ([2, 3, 4, 5] if not attn else [0, 1, 2, 3, 4, 5])
            p.wait("tensor", s_zTe, s_zTe.n)
            p.wait("tensor", s_xm, 2 * n_att)
            for ci, c in enumerate(cols):
                bank = ci % 2
                p.wait("tensor", s_qkev, s_qkev.n)
                p.wait("tensor", s_vev, s_vev.n)
                for kk in range(8):
                    p.op("tensor", lambda e, k=kk, c=c, bank=bank: e.matmul(
                        qkv_ps[bank][:], zT[:, k * 128:(k + 1) * 128],
                        wqkv_sb[:, k * 3 * D + c * 512:k * 3 * D + (c + 1) * 512],
                        start=(k == 0), stop=(k == 7)),
                        sem=(s_qkv if kk == 7 else None))
                if c <= 1:
                    p.wait("vector", s_qkv, s_qkv.n)
                    p.wait("vector", sa, sa.n)
                    p.op("vector", lambda e, c=c, bank=bank: e.tensor_copy(
                        out=q_f[:, c * 512:(c + 1) * 512], in_=qkv_ps[bank][:]),
                        sem=s_qkev)
                elif c <= 3:
                    p.wait("vector", s_qkv, s_qkv.n)
                    p.wait("vector", sa, sa.n)
                    p.op("vector", lambda e, c=c, bank=bank: e.tensor_copy(
                        out=k_f[:, (c - 2) * 512:(c - 1) * 512], in_=qkv_ps[bank][:]),
                        sem=s_qkev)
                else:
                    p.wait("scalar", s_qkv, s_qkv.n)
                    p.wait("scalar", s_pv, s_pv.n)
                    p.op("scalar", lambda e, c=c, bank=bank, sl=slot: e.activation(
                        out=vcache[:, sl * D + (c - 4) * 512:sl * D + (c - 3) * 512],
                        in_=qkv_ps[bank][:], func=AF.Copy), sem=s_vev)

            def qk_proc(xf, x1buf, xfbf, tabc, tabs_, off):
                p.wait("scalar", s_qkev, s_qkev.n)
                p.wait("scalar", sv, sv.n)
                p.op("scalar", lambda e: e.activation(out=qsq[:], in_=xf[:],
                                                     func=AF.Square), sem=sa)
                p.wait("vector", sa, sa.n)
                p.op("vector", lambda e, o=off: e.reduce_sum(
                    out=ssq[:, o:o + HEADS],
                    in_=qsq[:].rearrange("p (h d) -> p h d", d=DH), axis=AX.X), sem=sv)
                p.wait("vector", sv, sv.n)
                p.op("vector", lambda e, o=off: e.tensor_scalar(
                    out=ssq[:, o:o + HEADS], in0=ssq[:, o:o + HEADS], scalar1=1e-24,
                    scalar2=None, op0=ALU.add), sem=sv)
                p.wait("scalar", sv, sv.n)
                p.op("scalar", lambda e, o=off: e.activation(
                    out=ssq[:, o:o + HEADS], in_=ssq[:, o:o + HEADS], func=AF.Ln),
                    sem=sa)
                p.wait("scalar", sa, sa.n)
                p.op("scalar", lambda e, o=off: e.activation(
                    out=ssq[:, o:o + HEADS], in_=ssq[:, o:o + HEADS], func=AF.Exp,
                    scale=-0.5), sem=sa)
                p.wait("vector", sa, sa.n)
                p.op("vector", lambda e, o=off: e.tensor_tensor(
                    out=x1buf[:].rearrange("p (h d) -> p h d", d=DH),
                    in0=xf[:].rearrange("p (h d) -> p h d", d=DH),
                    in1=ssq[:, o:o + HEADS].rearrange("p (h o) -> p h o", o=1)
                        .broadcast_to([128, HEADS, DH]),
                    op=ALU.mult), sem=sv)
                p.wait("scalar", sv, sv.n)
                p.wait("scalar", s_qproc, s_qproc.n)
                x13 = x1buf[:].rearrange("p (h d) -> p h d", d=DH)
                rot3 = rot[:].rearrange("p (h d) -> p h d", d=DH)
                p.op("scalar", lambda e, x13=x13, rot3=rot3: e.activation(
                    out=rot3[:, :, 0:32], in_=x13[:, :, 32:64], func=AF.Copy,
                    scale=-1.0), sem=sa)
                p.op("scalar", lambda e, x13=x13, rot3=rot3: e.activation(
                    out=rot3[:, :, 32:64], in_=x13[:, :, 0:32], func=AF.Copy), sem=sa)
                t3c = tab_sb[tabc][:, j * DH:(j + 1) * DH].rearrange(
                    "p (o d) -> p o d", o=1).broadcast_to([128, HEADS, DH])
                t3s = tab_sb[tabs_][:, j * DH:(j + 1) * DH].rearrange(
                    "p (o d) -> p o d", o=1).broadcast_to([128, HEADS, DH])
                p.wait("vector", sv, sv.n)
                p.wait("vector", s_qproc, s_qproc.n)
                if j == 0:
                    p.wait("vector", s_tfold, 4)
                p.op("vector", lambda e, t3=t3c: e.tensor_tensor(
                    out=tq1[:].rearrange("p (h d) -> p h d", d=DH),
                    in0=x1buf[:].rearrange("p (h d) -> p h d", d=DH),
                    in1=t3, op=ALU.mult), sem=sv)
                p.wait("vector", sa, sa.n)
                p.wait("vector", sv, sv.n)
                p.op("vector", lambda e, t3=t3s: e.tensor_tensor(
                    out=rot[:].rearrange("p (h d) -> p h d", d=DH),
                    in0=rot[:].rearrange("p (h d) -> p h d", d=DH),
                    in1=t3, op=ALU.mult), sem=sv)
                p.wait("vector", sv, sv.n)
                st_T = s_qT if off == 0 else s_kT
                p.wait("vector", st_T, st_T.n)
                p.op("vector", lambda e, d=xfbf: e.tensor_tensor(
                    out=d[:], in0=tq1[:], in1=rot[:], op=ALU.add), sem=s_qproc)

            if attn:
                qk_proc(q_f, q1, qf_bf, "cs", "ss", 0)
            qk_proc(k_f, k1, kf_bf, "ci", "si", HEADS)

            if attn:
                p.wait("tensor", s_qproc, s_qproc.n - 1)
                p.wait("tensor", s_qTe, s_qTe.n)
                p.wait("tensor", s_zTe, s_zTe.n)
                for kk in range(8):
                    p.op("tensor", lambda e, k=kk: e.transpose(
                        tp_ps[:, k * 128:(k + 1) * 128], qf_bf[:, k * 128:(k + 1) * 128],
                        id_sb[:]), sem=(s_qT if kk == 7 else None))
                p.wait("scalar", s_qT, s_qT.n)
                p.wait("scalar", s_sim, s_sim.n)
                p.op("scalar", lambda e: e.activation(out=qT[:], in_=tp_ps[:],
                                                      func=AF.Copy), sem=s_qTe)
            p.wait("tensor", s_qproc, s_qproc.n)
            p.wait("tensor", s_kTe, s_kTe.n)
            p.wait("tensor", s_pte, s_pte.n)
            for kk in range(8):
                p.op("tensor", lambda e, k=kk: e.transpose(
                    pt_ps[:, k * 128:(k + 1) * 128], kf_bf[:, k * 128:(k + 1) * 128],
                    id_sb[:]), sem=(s_kT if kk == 7 else None))
            p.wait("vector", s_kT, s_kT.n)
            p.wait("vector", s_sim, s_sim.n)
            p.op("vector", lambda e, sl=slot: e.tensor_copy(
                out=kcache[:].rearrange("p (hp c) -> p hp c", c=512)[:, :, sl * 128:(sl + 1) * 128],
                in_=pt_ps[:].rearrange("p (hp c) -> p hp c", c=128)), sem=s_kTe)

            if attn:
                mclass = (4 if j == n_halo else 5 if j == n_halo + 1 else slot)
                moff = mclass * 512
                for h in range(HEADS):
                    par = h % 2
                    hp = h // 2
                    p.wait("tensor", s_qTe, s_qTe.n)
                    p.wait("tensor", s_kTe, s_kTe.n)
                    if s_sim.n >= 2:
                        p.wait("tensor", s_exp, s_exp.n - 1)
                    p.op("tensor", lambda e, par=par, hp=hp: e.matmul(
                        sim_ps[par][:], qT[64 * par:64 * (par + 1), hp * 128:(hp + 1) * 128],
                        kcache[64 * par:64 * (par + 1), hp * 512:(hp + 1) * 512],
                        start=True, stop=True), sem=s_sim)
                    p.wait("vector", s_sim, s_sim.n)
                    if s_mask.n >= 2:
                        pass
                    p.op("vector", lambda e, par=par, mo=moff: e.tensor_tensor(
                        out=sim_ps[par][:], in0=sim_ps[par][:],
                        in1=mask_sb[:, mo:mo + 512], op=ALU.add), sem=s_mask)
                    p.wait("vector", s_mask, s_mask.n)
                    p.wait("vector", s_exp, max(0, s_exp.n - 1))
                    p.op("vector", lambda e, par=par: e.reduce_max(
                        out=nmax[par][:], in_=sim_ps[par][:], axis=AX.X, negate=True),
                        sem=s_max)
                    p.wait("scalar", s_max, s_max.n)
                    p.wait("scalar", s_pt, s_pt.n)
                    p.wait("scalar", s_rcp, max(0, s_rcp.n - 1))
                    p.op("scalar", lambda e, par=par: e.activation(
                        out=P_sb[par][:], in_=sim_ps[par][:], func=AF.Exp,
                        bias=nmax[par][:], accum_out=sume[par][:]), sem=s_exp)
                    p.wait("tensor", s_exp, s_exp.n)
                    p.wait("tensor", s_pte, max(0, s_pte.n - 1))
                    if h < 2:
                        p.wait("tensor", s_kTe, s_kTe.n)
                    for kk in range(4):
                        p.op("tensor", lambda e, par=par, k=kk: e.transpose(
                            pt_ps[:, par * 512 + k * 128:par * 512 + (k + 1) * 128],
                            P_sb[par][:, k * 128:(k + 1) * 128], id_sb[:]),
                            sem=(s_pt if kk == 3 else None))
                    p.wait("scalar", s_pt, s_pt.n)
                    p.wait("scalar", s_pv, max(0, s_pv.n - 1))
                    p.op("scalar", lambda e, par=par: e.activation(
                        out=ptT[par][:], in_=pt_ps[:, par * 512:(par + 1) * 512],
                        func=AF.Copy), sem=s_pte)
                    p.wait("vector", s_exp, s_exp.n)
                    p.wait("vector", s_ao, max(0, s_ao.n - 1))
                    p.op("vector", lambda e, par=par: e.reciprocal(
                        out=rs[par][:], in_=sume[par][:]), sem=s_rcp)
                    p.wait("tensor", s_pte, s_pte.n)
                    p.wait("tensor", s_ao, max(0, s_ao.n - 1))
                    p.wait("tensor", s_vev, s_vev.n)
                    for kk in range(4):
                        p.op("tensor", lambda e, par=par, k=kk, h=h: e.matmul(
                            pv_ps[par][:, 0:DH], ptT[par][:, k * 128:(k + 1) * 128],
                            vcache[:, k * D + h * DH:k * D + (h + 1) * DH],
                            start=(k == 0), stop=(k == 3)),
                            sem=(s_pv if kk == 3 else None))
                    p.wait("vector", s_pv, s_pv.n)
                    p.wait("vector", s_rcp, s_rcp.n)
                    if h == 0:
                        p.wait("vector", s_aoT, s_aoT.n)
                    p.op("vector", lambda e, par=par, h=h: e.tensor_scalar(
                        out=ao_sb[:, h * DH:(h + 1) * DH], in0=pv_ps[par][:, 0:DH],
                        scalar1=rs[par][:], scalar2=None, op0=ALU.mult), sem=s_ao)
                n_att += 1

                p.wait("tensor", s_ao, s_ao.n)
                p.wait("tensor", s_aoTe, s_aoTe.n)
                p.wait("tensor", s_qTe, s_qTe.n)
                for kk in range(8):
                    p.op("tensor", lambda e, k=kk: e.transpose(
                        tp_ps[:, k * 128:(k + 1) * 128], ao_sb[:, k * 128:(k + 1) * 128],
                        id_sb[:]), sem=(s_aoT if kk == 7 else None))
                p.wait("scalar", s_aoT, s_aoT.n)
                p.wait("scalar", s_womm, s_womm.n)
                p.op("scalar", lambda e: e.activation(out=aoT[:], in_=tp_ps[:],
                                                      func=AF.Copy), sem=s_aoTe)
                p.wait("tensor", s_aoTe, s_aoTe.n)
                p.wait("tensor", s_xm, 2 * max(0, n_att - 1))
                for c in range(2):
                    for kk in range(8):
                        p.op("tensor", lambda e, k=kk, c=c: e.matmul(
                            qkv_ps[c][:], aoT[:, k * 128:(k + 1) * 128],
                            wo_sb[:, k * D + c * 512:k * D + (c + 1) * 512],
                            start=(k == 0), stop=(k == 7)),
                            sem=(s_womm if kk == 7 else None))
                p.wait("vector", s_womm, s_womm.n)
                a_idx = n_att - 1
                p.wait("vector", s_st[a_idx % 2], 16 * (a_idx // 2))
                for c in range(2):
                    p.op("vector", lambda e, c=c, b=jb, na=n_att: e.tensor_tensor(
                        out=xmid_sb[(na - 1) % 2][:, c * 512:(c + 1) * 512],
                        in0=qkv_ps[c][:], in1=zal[b][:, c * 512:(c + 1) * 512],
                        op=ALU.add), sem=s_xm)
                p.wait("gpsimd", s_xm, s_xm.n)
                p.dma("gpsimd", xmidd[128 * (j - n_halo):128 * (j - n_halo + 1), :],
                      xmid_sb[a_idx % 2][:], sem=s_st[a_idx % 2])

        p.wait("sync", s_st[0], s_st[0].n)
        p.wait("sync", s_st[1], s_st[1].n)
        nc_built = p.build()
    return nc_built



def build_attnslot(n_own=N_OWN, n_pref=N_OWN):
    stack = ExitStack()
    with stack:
        p = Prog(stack)
        nc = p.nc
        T = n_own * 128
        TP = n_pref * 128
        xod = nc.dram_tensor("x_own", [T, D], F32, kind="ExternalInput").ap()
        xpd = nc.dram_tensor("x_pref", [TP, D], F32, kind="ExternalInput").ap()
        wwd = nc.dram_tensor("w_write", [D, D], F32, kind="ExternalInput").ap()
        wrd = nc.dram_tensor("w_read", [D, D], F32, kind="ExternalInput").ap()
        decayd = nc.dram_tensor("decay", [1, 1], F32, kind="ExternalInput").ap()
        logad = nc.dram_tensor("log_alpha", [1, 1], F32, kind="ExternalInput").ap()
        iprefd = nc.dram_tensor("iota_pref", [128, max(1, n_pref)], F32,
                                kind="ExternalInput").ap()
        igpd = nc.dram_tensor("iota_gp", [128, 2], F32, kind="ExternalInput").ap()
        igwd = nc.dram_tensor("iota_gw", [128, 2], F32, kind="ExternalInput").ap()
        diffsTd = nc.dram_tensor("diffsT", [256, 256], F32, kind="ExternalInput").ap()
        causalTd = nc.dram_tensor("causalT", [256, 256], F32, kind="ExternalInput").ap()
        identd = nc.dram_tensor("ident", [128, 128], BF16, kind="ExternalInput").ap()
        zrowd = nc.dram_tensor("zrow", [1, D], BF16, kind="ExternalInput").ap()
        x1d = nc.dram_tensor("x1", [T, D], F32, kind="ExternalOutput").ap()
        zd = nc.dram_tensor("z", [T, D], F32, kind="ExternalOutput").ap()

        NCH = n_own // 2

        u_sb = p.sbuf("u_sb", [128, max(n_own, n_pref, 8) * D], BF16)
        WT_f = p.sbuf("WT_f", [128, 8 * D], F32)
        WT_b = p.sbuf("WT_b", [128, 8 * D], BF16)
        WwrT = p.sbuf("WwrT", [128, 8 * D], BF16)
        WrdT = p.sbuf("WrdT", [128, 8 * D], BF16)
        id_sb = p.sbuf("id_sb", [128, 128], BF16)
        xt = [p.sbuf(f"xt{i}", [128, D], F32) for i in range(2)]
        t1 = p.sbuf("t1", [128, D], F32)
        stat = p.sbuf("stat", [128, 8], F32)
        stat2 = p.sbuf("stat2", [128, 8], F32)
        sc = p.sbuf("sc", [128, 12], F32)
        ipref = p.sbuf("ipref", [128, max(1, n_pref)], F32)
        gp_sb = p.sbuf("gp_sb", [128, 2], F32)
        gw_sb = p.sbuf("gw_sb", [128, 2], F32)
        spref = p.sbuf("spref", [128, 1], F32)
        MT_sb = p.sbuf("MT_sb", [128, 2 * 256], BF16)
        wk_sb = p.sbuf("wk_sb", [128, 2 * D], BF16)
        wkT_sb = p.sbuf("wkT_sb", [128, 2 * D], BF16)
        uT_sb = p.sbuf("uT_sb", [128, 2 * D], BF16)
        v_sb = p.sbuf("v_sb", [128, 2 * D], BF16)
        vgw_sb = p.sbuf("vgw_sb", [128, 2 * D], BF16)
        smt_sb = p.sbuf("smt_sb", [128, 2 * 256], BF16)
        causT = xt[0][:, 0:512]
        mtmp = xt[1][:, 0:256]
        reads_sb = p.sbuf("reads_sb", [128, 2 * D], BF16)
        readsT_sb = p.sbuf("rT_sb", [128, 2 * D], BF16)
        ao_sb = p.sbuf("ao_sb", [128, D], F32)
        out_sb = p.sbuf("o0", [128, 2 * D], F32)
        ulast = readsT_sb[0:1, 0:D]

        tp_ps = p.psum("tp", [128, 1024], BF16)
        pA = p.psum("pA", [128, 1024], F32)
        pB = p.psum("pB", [128, 1024], F32)
        pC = p.psum("pC", [128, 1024], F32)
        pS = p.psum("pS", [128, 512], F32)

        s_cload = p.sem()
        s_xld = [p.sem(), p.sem()]
        sv = p.sem()
        sa = p.sem()
        s_T = p.sem()
        s_Te = p.sem()
        s_mm = p.sem()
        s_ev = p.sem()
        s_sev = p.sem()
        s_u = p.sem()
        s_wk = [p.sem(), p.sem()]
        s_st = [p.sem(), p.sem()]
        s_ul = p.sem()
        s_sc = p.sem()

        wl = WLoader(p, "w")

        p.dma("sync", id_sb[:], identd, sem=s_cload)
        p.dma("sync", sc[:, 0:1], decayd.broadcast_to([128, 1]), sem=s_cload)
        p.dma("sync", sc[:, 1:2], logad.broadcast_to([128, 1]), sem=s_cload)
        p.dma("sync", ipref[:], iprefd, sem=s_cload)
        p.dma("sync", gp_sb[:], igpd, sem=s_cload)
        p.dma("sync", gw_sb[:], igwd, sem=s_cload)
        p.dma("sync", causT.rearrange("p (t c) -> p t c", c=256),
              causalTd.rearrange("(t p) c -> p t c", p=128), sem=s_cload)
        n_cload = 7 * 16

        p.wait("scalar", s_cload, n_cload)
        p.op("scalar", lambda e: e.activation(out=sc[:, 6:7], in_=sc[:, 0:1],
                                              func=AF.Exp, scale=-1.0), sem=sa)
        p.wait("vector", sa, sa.n)
        p.op("vector", lambda e: e.tensor_scalar(out=sc[:, 6:7], in0=sc[:, 6:7],
                                                 scalar1=1.0, scalar2=None,
                                                 op0=ALU.add), sem=sv)
        p.wait("vector", sv, sv.n)
        p.op("vector", lambda e: e.reciprocal(out=sc[:, 2:3], in_=sc[:, 6:7]), sem=sv)
        p.wait("scalar", sv, sv.n)
        p.op("scalar", lambda e: e.activation(out=sc[:, 3:4], in_=sc[:, 2:3],
                                              func=AF.Ln), sem=sa)
        p.wait("scalar", sa, sa.n)
        p.op("scalar", lambda e: e.activation(out=sc[:, 4:5], in_=sc[:, 3:4],
                                              func=AF.Exp, scale=256.0), sem=sa)
        p.op("scalar", lambda e: e.activation(out=sc[:, 5:6], in_=sc[:, 1:2],
                                              func=AF.Exp), sem=sa)
        p.op("scalar", lambda e: e.activation(out=gp_sb[:], in_=gp_sb[:],
                                              func=AF.Exp, scale=sc[:, 3:4]), sem=sa)
        p.op("scalar", lambda e: e.activation(out=gw_sb[:], in_=gw_sb[:],
                                              func=AF.Exp, scale=sc[:, 3:4]), sem=sa)
        s_mld = p.sem()
        for tt in range(2):
            if tt > 0:
                p.wait("sync", sa, sa.n)
                p.wait("sync", sv, sv.n)
            p.dma("sync", mtmp, diffsTd[tt * 128:(tt + 1) * 128, :], sem=s_mld)
            p.wait("scalar", s_cload, n_cload)
            p.wait("scalar", s_mld, 16 * (tt + 1))
            p.wait("scalar", sv, sv.n)
            p.op("scalar", lambda e, tt=tt: e.activation(
                out=mtmp, in_=mtmp, func=AF.Exp, scale=sc[:, 3:4]), sem=sa)
            p.wait("vector", sa, sa.n)
            p.op("vector", lambda e, tt=tt: e.tensor_tensor(
                out=MT_sb[:, tt * 256:(tt + 1) * 256], in0=mtmp,
                in1=causT[:, tt * 256:(tt + 1) * 256], op=ALU.mult), sem=sv)
        n_sc = sa.n

        wnat = u_sb
        def build_T(dst, srcd):
            wl.load_matrix(lambda kk: wnat[:, kk * D:(kk + 1) * D], srcd, D, D)
            nW = wl.s_cast.n
            for dd in range(8):
                p.wait("tensor", wl.s_cast, nW)
                p.wait("tensor", s_Te, s_Te.n)
                for ee in range(8):
                    p.op("tensor", lambda e, dd=dd, ee=ee: e.transpose(
                        tp_ps[:, ee * 128:(ee + 1) * 128],
                        wnat[:, ee * D + dd * 128:ee * D + (dd + 1) * 128], id_sb[:]),
                        sem=(s_T if ee == 7 else None))
                p.wait("scalar", s_T, s_T.n)
                p.op("scalar", lambda e, dst=dst, dd=dd: e.activation(
                    out=dst[:, dd * D:(dd + 1) * D], in_=tp_ps[:], func=AF.Copy),
                    sem=s_Te)
        p.wait("scalar", s_sc, 0)
        build_T(WwrT, wwd)
        p.wait("sync", s_Te, s_Te.n)
        build_T(WrdT, wrd)

        def emit_u_tile(xd, i, dst_off):
            jb = i % 2
            p.wait("sync", s_u, s_u.n - 1)
            p.dma("sync", xt[jb][:], xd[128 * i:128 * (i + 1), :], sem=s_xld[jb])
            nld = s_xld[jb].n
            p.wait("vector", s_xld[jb], nld)
            p.wait("scalar", s_xld[jb], nld)
            emit_ln_stats(p, xt[jb][:], stat, sv, sa, t1[:], LN_EPS)
            p.wait("vector", sa, sa.n)
            p.op("vector", lambda e, b=jb, o=dst_off: e.tensor_scalar(
                out=u_sb[:, o:o + D], in0=xt[b][:], scalar1=stat[:, 2:3],
                scalar2=stat[:, 4:5], op0=ALU.subtract, op1=ALU.mult), sem=s_u)

        p.wait("vector", s_Te, s_Te.n)
        p.wait("sync", sv, sv.n)
        for i in range(n_pref):
            emit_u_tile(xpd, i, i * D)
        p.wait("gpsimd", s_u, s_u.n)
        p.dma("gpsimd", ulast, u_sb[127:128, (n_pref - 1) * D:n_pref * D],
              sem=s_ul)

        vs_sb = v_sb[:, 0:D]
        p.op("vector", lambda e: e.memset(WT_f[:], 0.0), sem=s_ev)
        for i in range(n_pref):
            ib = i % 2
            p.wait("gpsimd", s_mm, s_mm.n)
            p.dma("gpsimd", wk_sb[1:128, 0:D], u_sb[0:127, i * D:(i + 1) * D],
                  sem=s_wk[ib])
            if i == 0:
                p.dma("gpsimd", wk_sb[0:1, 0:D], zrowd, sem=s_wk[ib])
            else:
                p.dma("gpsimd", wk_sb[0:1, 0:D], u_sb[127:128, (i - 1) * D:i * D],
                      sem=s_wk[ib])
            p.wait("tensor", s_u, s_u.n)
            p.wait("tensor", s_Te, s_Te.n)
            for kk in range(8):
                p.op("tensor", lambda e, i=i, kk=kk: e.transpose(
                    tp_ps[:, kk * 128:(kk + 1) * 128],
                    u_sb[:, i * D + kk * 128:i * D + (kk + 1) * 128], id_sb[:]),
                    sem=(s_T if kk == 7 else None))
            p.wait("scalar", s_T, s_T.n)
            p.wait("scalar", s_mm, s_mm.n)
            p.op("scalar", lambda e: e.activation(out=uT_sb[:, 0:D], in_=tp_ps[:],
                                                  func=AF.Copy), sem=s_Te)
            p.wait("tensor", s_Te, s_Te.n)
            p.wait("tensor", s_ev, s_ev.n)
            for dd in range(8):
                for nn in range(2):
                    p.op("tensor", lambda e, dd=dd, nn=nn: e.matmul(
                        pA[:, nn * 512:(nn + 1) * 512],
                        uT_sb[:, dd * 128:(dd + 1) * 128],
                        WwrT[:, dd * D + nn * 512:dd * D + (nn + 1) * 512],
                        start=(dd == 0), stop=(dd == 7)),
                        sem=(s_mm if (dd == 7 and nn == 1) else None))
            p.wait("scalar", sv, sv.n)
            p.op("scalar", lambda e, i=i: e.activation(
                out=spref[:], in_=ipref[:, i:i + 1], func=AF.Exp,
                scale=sc[:, 3:4]), sem=sa)
            p.wait("vector", s_mm, s_mm.n)
            p.wait("vector", sa, sa.n)
            p.op("vector", lambda e: e.tensor_scalar(
                out=vs_sb, in0=pA[:], scalar1=spref[:], scalar2=None,
                op0=ALU.mult), sem=s_ev)
            p.wait("tensor", s_ev, s_ev.n)
            p.wait("tensor", s_wk[ib], s_wk[ib].n)
            for jsl in range(8):
                pc = pC if jsl % 2 == 0 else pB
                if i > 0 or jsl >= 2:
                    p.wait("tensor", s_ev, s_ev.n)
                for nn in range(2):
                    p.op("tensor", lambda e, jsl=jsl, pc=pc, nn=nn: e.matmul(
                        pc[:, nn * 512:(nn + 1) * 512],
                        wk_sb[:, jsl * 128:(jsl + 1) * 128],
                        vs_sb[:, nn * 512:(nn + 1) * 512],
                        start=True, stop=True),
                        sem=(s_mm if nn == 1 else None))
                p.wait("vector", s_mm, s_mm.n)
                p.op("vector", lambda e, jsl=jsl, pc=pc: e.scalar_tensor_tensor(
                    out=WT_f[:, jsl * D:(jsl + 1) * D], in0=pc[:], scalar=1.0,
                    in1=WT_f[:, jsl * D:(jsl + 1) * D], op0=ALU.mult, op1=ALU.add),
                    sem=s_ev)
        for jsl in range(8):
            p.wait("scalar", s_ev, s_ev.n)
            p.op("scalar", lambda e, jsl=jsl: e.activation(
                out=WT_b[:, jsl * D:(jsl + 1) * D], in_=WT_f[:, jsl * D:(jsl + 1) * D],
                func=AF.Copy), sem=s_sev)

        for i in range(n_own):
            p.wait("vector", s_mm, s_mm.n)
            p.wait("vector", s_ul, 16)
            p.wait("vector", s_wk[0], s_wk[0].n)
            p.wait("vector", s_wk[1], s_wk[1].n)
            emit_u_tile(xod, i, i * D)

        for c in range(NCH):
            tA, tB = 2 * c, 2 * c + 1
            cb = c % 2
            p.wait("gpsimd", s_mm, s_mm.n)
            if c == 0:
                p.wait("gpsimd", s_ul, 16)
                p.wait("gpsimd", s_u, s_u.n)
            for th, tt in ((0, tA), (1, tB)):
                p.dma("gpsimd", wk_sb[1:128, th * D:th * D + D],
                      u_sb[0:127, tt * D:(tt + 1) * D], sem=s_wk[cb])
                if tt == 0:
                    p.dma("gpsimd", wk_sb[0:1, 0:D], ulast, sem=s_wk[cb])
                else:
                    p.dma("gpsimd", wk_sb[0:1, th * D:th * D + D],
                          u_sb[127:128, (tt - 1) * D:tt * D], sem=s_wk[cb])
            p.wait("tensor", s_u, s_u.n)
            p.wait("tensor", s_Te, s_Te.n)
            p.wait("tensor", s_wk[cb], s_wk[cb].n)
            for th, tt in ((0, tA), (1, tB)):
                for kk in range(8):
                    p.op("tensor", lambda e, th=th, tt=tt, k=kk: e.transpose(
                        tp_ps[:, k * 128:(k + 1) * 128],
                        u_sb[:, tt * D + k * 128:tt * D + (k + 1) * 128], id_sb[:]),
                        sem=(s_T if kk == 7 else None))
                p.wait("scalar", s_T, s_T.n)
                p.wait("scalar", s_mm, s_mm.n)
                p.op("scalar", lambda e, th=th: e.activation(
                    out=uT_sb[:, th * D:(th + 1) * D], in_=tp_ps[:], func=AF.Copy),
                    sem=s_Te)
                p.wait("tensor", s_Te, s_Te.n)
                for kk in range(8):
                    p.op("tensor", lambda e, th=th, k=kk: e.transpose(
                        tp_ps[:, k * 128:(k + 1) * 128],
                        wk_sb[:, th * D + k * 128:th * D + (k + 1) * 128], id_sb[:]),
                        sem=(s_T if kk == 7 else None))
                p.wait("scalar", s_T, s_T.n)
                p.wait("scalar", s_mm, s_mm.n)
                p.op("scalar", lambda e, th=th: e.activation(
                    out=wkT_sb[:, th * D:(th + 1) * D], in_=tp_ps[:], func=AF.Copy),
                    sem=s_Te)
                p.wait("tensor", s_Te, s_Te.n)

            for th in range(2):
                p.wait("tensor", s_ev, s_ev.n)
                for dd in range(8):
                    for nn in range(2):
                        p.op("tensor", lambda e, th=th, dd=dd, nn=nn: e.matmul(
                            pA[:, nn * 512:(nn + 1) * 512],
                            uT_sb[:, th * D + dd * 128:th * D + (dd + 1) * 128],
                            WwrT[:, dd * D + nn * 512:dd * D + (nn + 1) * 512],
                            start=(dd == 0), stop=(dd == 7)),
                            sem=(s_mm if (dd == 7 and nn == 1) else None))
                p.wait("vector", s_mm, s_mm.n)
                p.wait("vector", s_T, s_T.n)
                p.op("vector", lambda e, th=th: e.tensor_copy(
                    out=v_sb[:, th * D:(th + 1) * D], in_=pA[:]), sem=s_ev)
                p.wait("vector", s_ev, s_ev.n)
                p.op("vector", lambda e, th=th: e.tensor_scalar(
                    out=vgw_sb[:, th * D:(th + 1) * D], in0=v_sb[:, th * D:(th + 1) * D],
                    scalar1=gw_sb[:, th:th + 1], scalar2=None, op0=ALU.mult), sem=s_ev)

            for tp_ in range(2):
                p.wait("tensor", s_ev, s_ev.n)
                p.wait("tensor", s_sev, s_sev.n)
                for tq in range(2):
                    pb = pS if tq == 0 else pB
                    for dd in range(8):
                        p.op("tensor", lambda e, tp_=tp_, dd=dd, tq=tq, pb=pb: e.matmul(
                            pb[:, 0:128],
                            wkT_sb[:, tp_ * D + dd * 128:tp_ * D + (dd + 1) * 128],
                            uT_sb[:, tq * D + dd * 128:tq * D + (dd + 1) * 128],
                            start=(dd == 0), stop=(dd == 7)),
                            sem=(s_mm if (dd == 7 and tq == 1) else None))
                p.wait("vector", s_mm, s_mm.n)
                for tq in range(2):
                    pb = pS if tq == 0 else pB
                    p.op("vector", lambda e, tp_=tp_, tq=tq, pb=pb: e.tensor_tensor(
                        out=smt_sb[:, tp_ * 256 + tq * 128:tp_ * 256 + (tq + 1) * 128],
                        in0=pb[:, 0:128],
                        in1=MT_sb[:, tp_ * 256 + tq * 128:tp_ * 256 + (tq + 1) * 128],
                        op=ALU.mult), sem=s_ev)

            for th, tt in ((0, tA), (1, tB)):
                p.wait("tensor", s_ev, s_ev.n)
                p.wait("tensor", s_sev, s_sev.n)
                for tp_ in range(2):
                    for nn in range(2):
                        p.op("tensor", lambda e, tp_=tp_, th=th, nn=nn: e.matmul(
                            pB[:, nn * 512:(nn + 1) * 512],
                            smt_sb[:, tp_ * 256 + th * 128:tp_ * 256 + (th + 1) * 128],
                            v_sb[:, tp_ * D + nn * 512:tp_ * D + (nn + 1) * 512],
                            start=(tp_ == 0), stop=(tp_ == 1)),
                            sem=(s_mm if (tp_ == 1 and nn == 1) else None))
                for dd in range(8):
                    for nn in range(2):
                        p.op("tensor", lambda e, th=th, dd=dd, nn=nn: e.matmul(
                            pA[:, nn * 512:(nn + 1) * 512],
                            uT_sb[:, th * D + dd * 128:th * D + (dd + 1) * 128],
                            WT_b[:, dd * D + nn * 512:dd * D + (nn + 1) * 512],
                            start=(dd == 0), stop=(dd == 7)),
                            sem=(s_mm if (dd == 7 and nn == 1) else None))
                p.wait("vector", s_mm, s_mm.n)
                p.op("vector", lambda e: e.tensor_copy(out=t1[:], in_=pB[:]), sem=s_ev)
                p.wait("vector", s_ev, s_ev.n)
                p.op("vector", lambda e, th=th: e.scalar_tensor_tensor(
                    out=reads_sb[:, th * D:(th + 1) * D], in0=pA[:],
                    scalar=gp_sb[:, th:th + 1], in1=t1[:], op0=ALU.mult,
                    op1=ALU.add), sem=s_ev)
                p.wait("tensor", s_ev, s_ev.n)
                p.wait("tensor", s_Te, s_Te.n)
                for kk in range(8):
                    p.op("tensor", lambda e, th=th, k=kk: e.transpose(
                        tp_ps[:, k * 128:(k + 1) * 128],
                        reads_sb[:, th * D + k * 128:th * D + (k + 1) * 128], id_sb[:]),
                        sem=(s_T if kk == 7 else None))
                p.wait("scalar", s_T, s_T.n)
                p.wait("scalar", s_mm, s_mm.n)
                p.op("scalar", lambda e, th=th: e.activation(
                    out=readsT_sb[:, th * D:(th + 1) * D], in_=tp_ps[:], func=AF.Copy),
                    sem=s_Te)
                p.wait("tensor", s_Te, s_Te.n)
                p.wait("tensor", s_ev, s_ev.n)
                for ii in range(8):
                    for nn in range(2):
                        p.op("tensor", lambda e, th=th, ii=ii, nn=nn: e.matmul(
                            pB[:, nn * 512:(nn + 1) * 512],
                            readsT_sb[:, th * D + ii * 128:th * D + (ii + 1) * 128],
                            WrdT[:, ii * D + nn * 512:ii * D + (nn + 1) * 512],
                            start=(ii == 0), stop=(ii == 7)),
                            sem=(s_mm if (ii == 7 and nn == 1) else None))
                p.wait("vector", s_mm, s_mm.n)
                p.wait("vector", sa, sa.n)
                p.op("vector", lambda e, tt=tt: e.scalar_tensor_tensor(
                    out=ao_sb[:], in0=pB[:], scalar=sc[:, 5:6],
                    in1=u_sb[:, tt * D:(tt + 1) * D], op0=ALU.mult, op1=ALU.add),
                    sem=s_ev)
                ob = out_sb
                p.wait("sync", s_st[0], 32 * tt)
                p.wait("sync", s_u, s_u.n)
                p.dma("sync", xt[tt % 2][:], xod[128 * tt:128 * (tt + 1), :],
                      sem=s_xld[tt % 2])
                p.wait("vector", s_ev, s_ev.n)
                p.wait("scalar", s_ev, s_ev.n)
                emit_ln_stats(p, ao_sb[:], stat2, sv, sa, t1[:], LN_EPS)
                p.wait("vector", sa, sa.n)
                p.op("vector", lambda e: e.tensor_scalar(
                    out=t1[:], in0=ao_sb[:], scalar1=stat2[:, 2:3],
                    scalar2=stat2[:, 4:5], op0=ALU.subtract, op1=ALU.mult), sem=sv)
                nxl = s_xld[tt % 2].n
                p.wait("vector", s_xld[tt % 2], nxl)
                p.wait("vector", sv, sv.n)
                p.wait("vector", s_st[0], 32 * tt)
                p.op("vector", lambda e, tt=tt, ob=ob: e.tensor_tensor(
                    out=ob[:, 0:D], in0=t1[:], in1=xt[tt % 2][:], op=ALU.add), sem=sv)
                p.wait("vector", sv, sv.n)
                p.wait("scalar", sv, sv.n)
                emit_ln_stats(p, ob[:, 0:D], stat2, sv, sa, t1[:], LN_EPS)
                p.wait("vector", sa, sa.n)
                p.op("vector", lambda e, ob=ob: e.tensor_scalar(
                    out=ob[:, D:2 * D], in0=ob[:, 0:D], scalar1=stat2[:, 2:3],
                    scalar2=stat2[:, 4:5], op0=ALU.subtract, op1=ALU.mult), sem=sv)
                p.wait("gpsimd", sv, sv.n)
                p.dma("gpsimd", x1d[128 * tt:128 * (tt + 1), :], ob[:, 0:D],
                      sem=s_st[0])
                p.dma("gpsimd", zd[128 * tt:128 * (tt + 1), :], ob[:, D:2 * D],
                      sem=s_st[0])

            if c < NCH - 1:
                for jt in range(8):
                    p.wait("tensor", s_sev, s_sev.n)
                    p.wait("tensor", s_ev, s_ev.n)
                    for th in range(2):
                        for nn in range(2):
                            p.op("tensor", lambda e, jt=jt, th=th, nn=nn: e.matmul(
                                pC[:, nn * 512:(nn + 1) * 512],
                                wk_sb[:, th * D + jt * 128:th * D + (jt + 1) * 128],
                                vgw_sb[:, th * D + nn * 512:th * D + (nn + 1) * 512],
                                start=(th == 0), stop=(th == 1)),
                                sem=(s_mm if (th == 1 and nn == 1) else None))
                    p.wait("vector", s_mm, s_mm.n)
                    p.wait("vector", s_sev, s_sev.n)
                    p.op("vector", lambda e, jt=jt: e.scalar_tensor_tensor(
                        out=WT_f[:, jt * D:(jt + 1) * D], in0=WT_f[:, jt * D:(jt + 1) * D],
                        scalar=sc[:, 4:5], in1=pC[:], op0=ALU.mult, op1=ALU.add),
                        sem=s_ev)
                    p.wait("scalar", s_ev, s_ev.n)
                    p.op("scalar", lambda e, jt=jt: e.activation(
                        out=WT_b[:, jt * D:(jt + 1) * D], in_=WT_f[:, jt * D:(jt + 1) * D],
                        func=AF.Copy), sem=s_sev)

        p.wait("sync", s_st[0], s_st[0].n)
        p.wait("sync", s_st[1], s_st[1].n)
        nc_built = p.build()
    return nc_built



def _host_consts():
    import ml_dtypes
    ident = np.eye(128, dtype=ml_dtypes.bfloat16)
    pp = np.arange(128, dtype=np.float32)
    n_pref = N_OWN
    iota_pref = np.stack([(n_pref * 128 - 1) - 128 * i - pp for i in range(n_pref)], 1
                         ).astype(np.float32)
    iota_gp = np.stack([pp, 128 + pp], 1).astype(np.float32)
    iota_gw = np.stack([255 - pp, 127 - pp], 1).astype(np.float32)
    jj = np.arange(256)
    diffsT = np.clip(jj[None, :] - 1 - jj[:, None], 0, None).astype(np.float32)
    causalT = (jj[None, :] > jj[:, None]).astype(np.float32)
    zrow = np.zeros((1, D), ml_dtypes.bfloat16)
    return dict(ident=ident, iota_pref=iota_pref, iota_gp=iota_gp, iota_gw=iota_gw,
                diffsT=diffsT, causalT=causalT, zrow=zrow)


def _host_masks(h):
    masks = np.zeros((6, 128, 512), np.float32)
    ii = np.arange(128)[:, None]
    jj = np.arange(128)[None, :]
    tri = np.where(jj > ii, NEG, 0.0).astype(np.float32)
    atri = np.where(jj < ii, NEG, 0.0).astype(np.float32)
    for c in range(6):
        sl = c % 4
        masks[c, :, sl * 128:(sl + 1) * 128] += tri
        kill = (sl + 1) % 4
        masks[c, :, kill * 128:(kill + 1) * 128] = NEG
        anti = (sl + 2) % 4
        masks[c, :, anti * 128:(anti + 1) * 128] += atri
        if h == 0 and c >= 4:
            if c == 4:
                for s in (1, 2, 3):
                    masks[c, :, s * 128:(s + 1) * 128] = NEG
            else:
                for s in (2, 3):
                    masks[c, :, s * 128:(s + 1) * 128] = NEG
                masks[c, :, 0:128] = 0.0
    return masks.reshape(6 * 128, 512)


def _host_tabs(h):
    NT = N_HALO + N_OWN
    pos0 = h * T_OWN - N_HALO * 128
    pos = np.maximum(pos0 + np.arange(NT * 128), 0).astype(np.float32)
    half = np.arange(0, DH, 2, dtype=np.float32)
    inv_freq = 1.0 / 10000.0 ** (half / DH)
    freqs = pos[:, None] * inv_freq[None, :]
    freqs = np.concatenate([freqs, freqs], -1)
    base_scale = (half + 0.4 * DH) / (1.4 * DH)
    power = (pos - 8192 // 2) / (WINDOW // 2)
    scale = base_scale[None, :] ** power[:, None]
    scale = np.concatenate([scale, scale], -1)
    cs = (np.cos(freqs) * scale * QK_SCALE).astype(np.float32)
    ss = (np.sin(freqs) * scale * QK_SCALE).astype(np.float32)
    ci = (np.cos(freqs) / scale).astype(np.float32)
    si = (np.sin(freqs) / scale).astype(np.float32)
    return dict(cs=cs, ss=ss, ci=ci, si=si)


_CORES = [(b, h) for b in range(4) for h in range(2)]


def kernel(L0_attn_out, L0_attn_W_write, L0_attn_W_read, L0_attn_decay,
           L0_attn_log_alpha, L0_ffn_Wqkv, L0_ffn_q_scale, L0_ffn_k_scale,
           L0_ffn_Wo, L0_ffn_ln_g, L0_ffn_ln_b, L0_ffn_W1, L0_ffn_b1,
           L0_ffn_W2, L0_ffn_b2):
    x = np.ascontiguousarray(np.asarray(L0_attn_out, dtype=np.float32))
    B, T, _ = x.shape
    consts = _host_consts()
    masks = {h: _host_masks(h) for h in (0, 1)}
    tabs = {h: _host_tabs(h) for h in (0, 1)}
    cores = list(range(8))
    zeros_own = np.zeros((T_OWN, D), np.float32)

    nc_slot = build_attnslot()
    maps = []
    for (b, h) in _CORES:
        s = h * T_OWN
        maps.append(dict(
            x_own=x[b, s:s + T_OWN], x_pref=(x[b, 0:T_OWN] if h else zeros_own),
            w_write=np.asarray(L0_attn_W_write, np.float32),
            w_read=np.asarray(L0_attn_W_read, np.float32),
            decay=np.asarray(L0_attn_decay, np.float32).reshape(1, 1),
            log_alpha=np.asarray(L0_attn_log_alpha, np.float32).reshape(1, 1),
            iota_pref=consts["iota_pref"], iota_gp=consts["iota_gp"],
            iota_gw=consts["iota_gw"], diffsT=consts["diffsT"],
            causalT=consts["causalT"], ident=consts["ident"], zrow=consts["zrow"]))
    res = run_bass_kernel_spmd(nc_slot, maps, core_ids=cores)
    x1 = np.zeros((B, T, D), np.float32)
    z = np.zeros((B, T, D), np.float32)
    for i, (b, h) in enumerate(_CORES):
        s = h * T_OWN
        x1[b, s:s + T_OWN] = res.results[i]["x1"]
        z[b, s:s + T_OWN] = res.results[i]["z"]

    nc_attn = build_attnsub()
    nc_ffn = build_ffnsub(final=False)
    nc_ffnF = build_ffnsub(final=True)
    NT = N_HALO + N_OWN
    Wqkv = np.asarray(L0_ffn_Wqkv, np.float32)
    Wo = np.asarray(L0_ffn_Wo, np.float32)
    W1 = np.asarray(L0_ffn_W1, np.float32)
    W2 = np.asarray(L0_ffn_W2, np.float32)
    qsc = np.asarray(L0_ffn_q_scale, np.float32)
    ksc = np.asarray(L0_ffn_k_scale, np.float32)
    lng = np.asarray(L0_ffn_ln_g, np.float32)
    lnb = np.asarray(L0_ffn_ln_b, np.float32)
    b1 = np.asarray(L0_ffn_b1, np.float32)
    b2 = np.asarray(L0_ffn_b2, np.float32)
    out = None
    for lyr in range(6):
        maps = []
        for (b, h) in _CORES:
            s = h * T_OWN
            z_ext = np.zeros((NT * 128 + 1, D), np.float32)
            if h == 1:
                z_ext[0] = z[b, s - 513]
                z_ext[1:N_HALO * 128 + 1] = z[b, s - 512:s]
            z_ext[N_HALO * 128 + 1:] = z[b, s:s + T_OWN]
            maps.append(dict(
                z_ext=z_ext, wqkv=Wqkv[lyr], wo=Wo[lyr], qs=qsc[lyr][None],
                ks=ksc[lyr][None], masks=masks[h], ident=consts["ident"],
                **tabs[h]))
        res = run_bass_kernel_spmd(nc_attn, maps, core_ids=cores)
        xmid = np.zeros((B, T, D), np.float32)
        for i, (b, h) in enumerate(_CORES):
            s = h * T_OWN
            xmid[b, s:s + T_OWN] = res.results[i]["xmid"]
        final = (lyr == 5)
        maps = []
        for (b, h) in _CORES:
            s = h * T_OWN
            xme = np.zeros((T_OWN + 1, D), np.float32)
            if h == 1:
                xme[0] = xmid[b, s - 1]
            xme[1:] = xmid[b, s:s + T_OWN]
            m = dict(xmid=xme, w1=W1[lyr], w2=W2[lyr], lng=lng[lyr][None],
                     lnb=lnb[lyr][None], b1=b1[lyr][None], b2=b2[lyr][None],
                     ident=consts["ident"])
            if final:
                m["x1"] = x1[b, s:s + T_OWN]
            maps.append(m)
        res = run_bass_kernel_spmd(nc_ffnF if final else nc_ffn, maps, core_ids=cores)
        dst = np.zeros((B, T, D), np.float32)
        for i, (b, h) in enumerate(_CORES):
            s = h * T_OWN
            dst[b, s:s + T_OWN] = res.results[i]["zout"]
        if final:
            out = dst
        else:
            z = dst
    return out
